# version 03
# baseline (speedup 1.0000x reference)
import numpy as np
import ml_dtypes
import concourse.bass as bass
import concourse.mybir as mybir
from concourse.bass_utils import run_bass_kernel_spmd

F32 = mybir.dt.float32
BF16 = mybir.dt.bfloat16
I32 = mybir.dt.int32
AF = mybir.ActivationFunctionType
ALU = mybir.AluOpType
AX = mybir.AxisListType

T = 2048
NTT = 16
NFT = 8
NE = 8
NJP = 16
NH = 2
EPS = 1e-5

STAGE = 3
DEBUG_TAPS = False
LAST_RES = None

ENGS = ("tensor", "vector", "scalar", "sync", "gpsimd")
SEMS = ("s_ld", "s_g", "s_t", "s_v", "s_s")


class B:

    def __init__(self):
        self.ops = {e: [] for e in ENGS}
        self.cnt = {s: 0 for s in SEMS}
        self.lastw = {}

    def emit(self, eng, fn):
        self.ops[eng].append(fn)

    def wait(self, eng, sem, val):
        if val <= 0:
            return
        k = (eng, sem)
        if self.lastw.get(k, 0) >= val:
            return
        self.lastw[k] = val

        def f(E, S, sem=sem, val=val):
            E.wait_ge(S[sem], val)

        self.ops[eng].append(f)

    def inc(self, sem, by):
        self.cnt[sem] += by
        return self.cnt[sem]


def build_nc(stage=STAGE):
    nc = bass.Bass("TRN2", target_bir_lowering=False)

    x_in = nc.dram_tensor("x_in", [T, 1024], F32, kind="ExternalInput").ap()
    w13t = nc.dram_tensor("w13t", [NE, NJP, 128, 2, 8, 128], F32, kind="ExternalInput").ap()
    w2t = nc.dram_tensor("w2t", [NE, 16, 128, 1024], F32, kind="ExternalInput").ap()
    wqt = nc.dram_tensor("wqt", [1024, 1024], F32, kind="ExternalInput").ap()
    wkt = nc.dram_tensor("wkt", [1024, 1024], F32, kind="ExternalInput").ap()
    wvt = nc.dram_tensor("wvt", [1024, 1024], F32, kind="ExternalInput").ap()
    wot = nc.dram_tensor("wot", [1024, 1024], F32, kind="ExternalInput").ap()
    gwt = nc.dram_tensor("gwt", [1024, 8], F32, kind="ExternalInput").ap()
    nonce_in = nc.dram_tensor("nonce", [1, 1], I32, kind="ExternalInput").ap()
    y_out = nc.dram_tensor("y_out", [T, 1024], F32, kind="ExternalOutput").ap()
    ident_d = nc.inline_tensor(np.eye(128, dtype=np.float32), "ident_c").ap()

    sh_s = nc.dram_tensor("sh_s", [2, 128, 1024], F32, kind="Internal", addr_space="Shared").ap()
    sh_f = nc.dram_tensor("sh_f", [2, 1], I32, kind="Internal", addr_space="Shared").ap()

    O = 20992

    def at(name, shape, dtype, off):
        return nc.alloc_sbuf_tensor_at(name, shape, dtype, offset=O + off).ap()

    x_res = at("x_res", [128, NTT * 1024], F32, 0)
    xnT = at("xnT", [128, NFT * 2048], BF16, 65536)
    actT = at("actT", [128, NJP * 1024], BF16, 98304)
    w2tb = at("w2tb", [128, 16 * 1024], BF16, 131072)
    w13stage = at("w13stage", [128, 2048], F32, 163840)
    w2stage = at("w2stage", [128, 2 * 1024], F32, 172032)
    w13b = at("w13b", [128, 2 * 2048], BF16, 180224)
    ident = at("ident", [128, 128], F32, 188416)
    xsq = at("xsq", [128, 1024], F32, 188928)
    gstat = at("gstat", [128, 8 * 16], F32, 188928)
    gbuf = at("gbuf", [128, 4 * 128], F32, 189440)
    gtmp4 = at("gtmp4", [128, 3 * 64], F32, 191488)
    xn_tok = at("xn_tok", [128, 1024], F32, 193024)
    s_sb = at("s_sb", [128, 1024], F32, 193024)
    silu_t = at("silu_t", [128, 1024], BF16, 197120)
    s_bf = at("s_bf", [128, 1024], BF16, 197120)
    gate_sb = at("gate_sb", [128, 128], F32, 199168)
    combine = at("combine", [128, 128], F32, 200192)
    vtm = at("vtm", [128, 3 * 128], F32, 202240)
    ktm = at("ktm", [128, 3 * 128], F32, 203776)
    lnstat = at("lnstat", [128, 128], F32, 201216)
    gwstage = at("gwstage", [128, 64], F32, 201728)
    gwtb = at("gwtb", [128, 64], BF16, 201984)
    sb_nonce = at("sb_nonce", [1, 1], I32, 202112)
    sb_flag = at("sb_flag", [1, 1], I32, 202176)

    pA = nc.alloc_psum_tensor("pA", [128, 2048], F32).ap()
    pC = nc.alloc_psum_tensor("pC", [128, 2048], F32).ap()

    b = B()
    M = {}
    x_last_v = {tt: 0 for tt in range(NTT)}

    ld_after = {}

    def sdma(out_ap, in_ap, tag=None):
        v = b.inc("s_ld", 16)

        def f(E, S, out_ap=out_ap, in_ap=in_ap):
            E.dma_start(out=out_ap, in_=in_ap).then_inc(S["s_ld"], 16)

        b.emit("sync", f)
        if tag is not None:
            ld_after[tag] = v
        return v

    x_in3 = x_in.rearrange("(t p) f -> t p f", p=128)
    xres3 = x_res.rearrange("p (t f) -> p t f", f=1024)
    for g in range(4):
        sdma(xres3[:, 4 * g:4 * g + 4, :],
             x_in3[4 * g:4 * g + 4].rearrange("t p f -> p t f"), tag=("x", g))
    sdma(ident, ident_d, tag="ident")
    sdma(gwstage.rearrange("p (f e) -> p f e", e=8),
         gwt.rearrange("(f p) e -> p f e", p=128), tag="gw")
    sdma(sb_nonce, nonce_in, tag="nonce")

    b.wait("scalar", "s_ld", ld_after["nonce"])
    M["gw_cast"] = b.inc("s_s", 1)

    def f_gw(E, S):
        E.copy(gwtb, gwstage).then_inc(S["s_s"], 1)

    b.emit("scalar", f_gw)

    def x_tile(tt):
        return x_res[:, tt * 1024:(tt + 1) * 1024]

    def vop(build, deps=()):
        for d in deps:
            b.wait("vector", "s_v", d)
        v = b.inc("s_v", 1)

        def f(E, S, build=build):
            build(E).then_inc(S["s_v"], 1)

        b.emit("vector", f)
        return v

    def emit_ln(x_ready_v):
        evict_val = {}
        tr_val = {}
        b.wait("scalar", "s_v", x_ready_v)
        b.wait("vector", "s_ld", ld_after["nonce"])
        b.wait("scalar", "s_ld", ld_after["nonce"])
        b.wait("tensor", "s_ld", ld_after["nonce"])
        s_sums = {}
        for tt in range(NTT):
            sv = b.inc("s_s", 2)
            s_sums[tt] = sv

            def f_sums(E, S, tt=tt):
                E.activation(xsq, x_tile(tt), AF.Square,
                             accum_out=lnstat[:, 16 + tt:16 + tt + 1])
                E.activation(xsq, x_tile(tt), AF.Copy,
                             accum_out=lnstat[:, tt:tt + 1]).then_inc(S["s_s"], 2)

            b.emit("scalar", f_sums)
        b.wait("vector", "s_s", s_sums[NTT - 1])
        Lc = lnstat
        v_u = vop(lambda E: E.tensor_tensor(Lc[:, 32:48], Lc[:, 0:16], Lc[:, 0:16], ALU.mult))
        v_w = vop(lambda E: E.scalar_tensor_tensor(Lc[:, 48:64], Lc[:, 32:48], -1.0 / 1024.0,
                                                   Lc[:, 16:32], ALU.mult, ALU.add), deps=(v_u,))
        v_w2 = vop(lambda E: E.tensor_scalar(Lc[:, 64:80], Lc[:, 48:64], 1.0 / 1024.0, EPS,
                                             ALU.mult, ALU.add), deps=(v_w,))
        b.wait("scalar", "s_v", v_w2)
        sv2 = b.inc("s_s", 1)

        def f_sqrt(E, S):
            E.activation(Lc[:, 80:96], Lc[:, 64:80], AF.Sqrt).then_inc(S["s_s"], 1)

        b.emit("scalar", f_sqrt)
        b.wait("vector", "s_s", sv2)
        v_r = vop(lambda E: E.reciprocal(Lc[:, 96:112], Lc[:, 80:96]))
        v_n = vop(lambda E: E.tensor_scalar(Lc[:, 112:128], Lc[:, 0:16], -1.0 / 1024.0,
                                            None, ALU.mult))
        for tt in range(NTT):
            if tt >= 1:
                b.wait("vector", "s_t", tr_val[(tt - 1, NFT - 1)])
            v_a1 = vop(lambda E, tt=tt: E.tensor_tensor(
                xn_tok, x_tile(tt),
                Lc[:, 112 + tt:112 + tt + 1].broadcast_to([128, 1024]), ALU.add),
                deps=(v_n, x_ready_v))
            av = vop(lambda E, tt=tt: E.tensor_tensor(
                xn_tok, xn_tok,
                Lc[:, 96 + tt:96 + tt + 1].broadcast_to([128, 1024]), ALU.mult),
                deps=(v_a1, v_r))
            for ft in range(NFT):
                b.wait("tensor", "s_v", av)
                idx = tt * NFT + ft
                if idx >= 2:
                    pidx = idx - 2
                    b.wait("tensor", "s_s", evict_val[(pidx // NFT, pidx % NFT)])
                tv = b.inc("s_t", 1)
                tr_val[(tt, ft)] = tv
                ps = pA[:, 0:128] if ft % 2 == 0 else pA[:, 512:640]

                def f_tr(E, S, ps=ps, ft=ft):
                    E.transpose(ps, xn_tok[:, ft * 128:(ft + 1) * 128], ident).then_inc(S["s_t"], 1)

                b.emit("tensor", f_tr)
                b.wait("scalar", "s_t", tv)
                ev = b.inc("s_s", 1)
                evict_val[(tt, ft)] = ev

                def f_ev(E, S, ps=ps, ft=ft, tt=tt):
                    E.copy(xnT[:, ft * 2048 + tt * 128: ft * 2048 + (tt + 1) * 128],
                           ps).then_inc(S["s_s"], 1)

                b.emit("scalar", f_ev)
        return b.cnt["s_s"]

    def emit_gate(ln_done_s, psum_free_v):
        b.wait("tensor", "s_s", ln_done_s)
        b.wait("tensor", "s_s", M["gw_cast"])
        b.wait("tensor", "s_v", psum_free_v)
        tv = b.inc("s_t", 1)

        def f_gmm(E, S):
            last = None
            for tt in range(NTT):
                for ft in range(NFT):
                    last = E.matmul(pC[:, tt * 8:(tt + 1) * 8],
                                    xnT[:, ft * 2048 + tt * 128: ft * 2048 + (tt + 1) * 128],
                                    gwtb[:, ft * 8:(ft + 1) * 8],
                                    start=(ft == 0), stop=(ft == NFT - 1))
            last.then_inc(S["s_t"], 1)

        b.emit("tensor", f_gmm)
        b.wait("scalar", "s_t", tv)
        gev = b.inc("s_s", 1)

        def f_gev(E, S):
            E.copy(gate_sb, pC[:, 0:128]).then_inc(S["s_s"], 1)

        b.emit("scalar", f_gev)
        M["gate_psum_free"] = gev

        gs3 = gate_sb.rearrange("p (t e) -> p t e", e=8)
        el = gbuf[:, 0:128]
        el3 = el.rearrange("p (t e) -> p t e", e=8)
        masked = gbuf[:, 128:256]
        m3 = masked.rearrange("p (t e) -> p t e", e=8)
        mk2 = gbuf[:, 256:384]
        mk23 = mk2.rearrange("p (t e) -> p t e", e=8)
        wsel = gbuf[:, 384:512]
        w3 = wsel.rearrange("p (t e) -> p t e", e=8)
        g4 = gtmp4[:, 0:64].rearrange("p (t e) -> p t e", e=4)
        eqg = gtmp4[:, 64:128].rearrange("p (t e) -> p t e", e=4)
        g4b = gtmp4[:, 128:192].rearrange("p (t e) -> p t e", e=4)
        stt_ = gstat

        def bc(col, n=8):
            return stt_[:, col * 16:(col + 1) * 16].unsqueeze(2).broadcast_to([128, 16, n])

        b.wait("vector", "s_s", gev)
        v1 = vop(lambda E: E.reduce_max(stt_[:, 0:16], gs3, axis=AX.X))
        v2 = vop(lambda E: E.tensor_tensor(m3, gs3, bc(0), ALU.subtract), deps=(v1,))
        b.wait("scalar", "s_v", v2)
        s1 = b.inc("s_s", 1)

        def f_exp(E, S):
            E.activation(el, masked, AF.Exp).then_inc(S["s_s"], 1)

        b.emit("scalar", f_exp)
        b.wait("vector", "s_s", s1)
        v3a = vop(lambda E: E.tensor_tensor(g4, el3[:, :, 0:8:2], el3[:, :, 1:8:2], ALU.max))
        v3 = vop(lambda E: E.reduce_max(stt_[:, 16:32], g4, axis=AX.X), deps=(v3a,))
        v4a = vop(lambda E: E.tensor_tensor(eqg, g4, bc(1, 4), ALU.is_equal), deps=(v3,))
        v4b = vop(lambda E: E.scalar_tensor_tensor(g4b, eqg, -1e30, g4, ALU.mult, ALU.add),
                  deps=(v4a,))
        v4 = vop(lambda E: E.reduce_max(stt_[:, 32:48], g4b, axis=AX.X), deps=(v4b,))
        v5a = vop(lambda E: E.tensor_tensor(eqg, g4, bc(2, 4), ALU.is_ge), deps=(v4,))
        v5b = vop(lambda E: E.tensor_tensor(m3[:, :, 0:8:2], el3[:, :, 0:8:2], eqg, ALU.mult),
                  deps=(v5a,))
        v5c = vop(lambda E: E.tensor_tensor(m3[:, :, 1:8:2], el3[:, :, 1:8:2], eqg, ALU.mult),
                  deps=(v5a,))
        v5 = vop(lambda E: E.reduce_max(stt_[:, 48:64], m3, axis=AX.X), deps=(v5b, v5c))
        v6a = vop(lambda E: E.tensor_tensor(mk23, m3, bc(3), ALU.is_equal), deps=(v5,))
        v6b = vop(lambda E: E.scalar_tensor_tensor(mk23, mk23, -1e30, m3, ALU.mult, ALU.add),
                  deps=(v6a,))
        v6 = vop(lambda E: E.reduce_max(stt_[:, 64:80], mk23, axis=AX.X), deps=(v6b,))
        v7a = vop(lambda E: E.tensor_tensor(mk23, m3, bc(4), ALU.is_ge), deps=(v6,))
        v7b = vop(lambda E: E.tensor_tensor(w3, m3, mk23, ALU.mult), deps=(v7a,))
        v7 = vop(lambda E: E.reduce_sum(stt_[:, 80:96], w3, axis=AX.X), deps=(v7b,))
        v8a = vop(lambda E: E.tensor_scalar(stt_[:, 96:112], stt_[:, 80:96], 2.0, None, ALU.mult),
                  deps=(v7,))
        v8 = vop(lambda E: E.reciprocal(stt_[:, 112:128], stt_[:, 96:112]), deps=(v8a,))
        v9 = vop(lambda E: E.tensor_tensor(combine.rearrange("p (t e) -> p t e", e=8), w3,
                                           bc(7), ALU.mult), deps=(v8,))
        M["combine_ready"] = v9

    k13 = {"cast": {}, "mm1": {}, "silu": {}, "act": {}, "k": 0}
    w2c = {}

    def emit_ffn(call):
        war_t = b.cnt["s_t"]
        war_s = b.cnt["s_s"]
        war_v = b.cnt["s_v"]
        k_start = k13["k"]
        last_resid = {0: war_v, 1: war_v}
        last_resid_any = war_v
        mm2_last = war_t
        w2_prev_cast = w2c.get("last", 0)

        def emit_silu_act(k, jp, P):
            b.wait("scalar", "s_t", k13["mm1"][k])
            if k - k_start >= 1:
                b.wait("scalar", "s_v", k13["act"][k - 1])
            lv = b.inc("s_s", 1)
            k13["silu"][k] = lv

            def f_si(E, S, P=P):
                E.activation(silu_t, P[:, 0:1024], AF.Silu).then_inc(S["s_s"], 1)

            b.emit("scalar", f_si)
            b.wait("vector", "s_s", lv)
            av = b.inc("s_v", 1)
            k13["act"][k] = av

            def f_ac(E, S, jp=jp, P=P):
                E.tensor_tensor(actT[:, jp * 1024:(jp + 1) * 1024], silu_t,
                                P[:, 1024:2048], ALU.mult).then_inc(S["s_v"], 1)

            b.emit("vector", f_ac)

        for e in range(NE):
            cast2 = {}
            pend = None
            for h in range(NH):
                for jp in range(NJP):
                    k = k13["k"]
                    if k - k_start >= 1:
                        b.wait("sync", "s_s", k13["cast"][k - 1])
                    else:
                        b.wait("sync", "s_t", war_t)
                        b.wait("sync", "s_s", war_s)
                    dv = b.inc("s_ld", 16)
                    bslot = w13b[:, (k % 2) * 2048:(k % 2) * 2048 + 2048]

                    def f_wd(E, S, e=e, jp=jp):
                        E.dma_start(out=w13stage,
                                    in_=w13t[e, jp].rearrange("p g f m -> p (g f m)")).then_inc(S["s_ld"], 16)

                    b.emit("sync", f_wd)
                    b.wait("scalar", "s_ld", dv)
                    if k - k_start >= 2:
                        b.wait("scalar", "s_t", k13["mm1"][k - 2])
                    else:
                        b.wait("scalar", "s_t", war_t)
                    cv = b.inc("s_s", 1)
                    k13["cast"][k] = cv

                    def f_wc(E, S, bslot=bslot):
                        E.copy(bslot, w13stage).then_inc(S["s_s"], 1)

                    b.emit("scalar", f_wc)
                    if h == 0:
                        j = jp
                        if j >= 2:
                            b.wait("gpsimd", "s_s", cast2[j - 2])
                        else:
                            b.wait("gpsimd", "s_s", w2_prev_cast if e == 0 else w2c[(call, e - 1, 15)])
                            if call > 0 and e == 0:
                                b.wait("gpsimd", "s_s", war_s)
                                b.wait("gpsimd", "s_t", war_t)
                        gv = b.inc("s_g", 16)
                        slot2 = w2stage[:, (j % 2) * 1024:(j % 2) * 1024 + 1024]

                        def f_d(E, S, e=e, j=j, slot2=slot2):
                            E.dma_start(out=slot2, in_=w2t[e, j]).then_inc(S["s_g"], 16)

                        b.emit("gpsimd", f_d)
                        b.wait("scalar", "s_g", gv)
                        b.wait("scalar", "s_t", mm2_last)
                        c2 = b.inc("s_s", 1)
                        cast2[j] = c2
                        w2c[(call, e, j)] = c2

                        def f_c(E, S, j=j, slot2=slot2):
                            E.copy(w2tb[:, j * 1024:(j + 1) * 1024], slot2).then_inc(S["s_s"], 1)

                        b.emit("scalar", f_c)
                    if pend is not None:
                        emit_silu_act(*pend)
                    b.wait("tensor", "s_s", cv)
                    if k - k_start >= 2:
                        b.wait("tensor", "s_s", k13["silu"][k - 2])
                        b.wait("tensor", "s_v", k13["act"][k - 2])
                    if jp <= 1:
                        b.wait("tensor", "s_v", last_resid_any)
                        if e == 0 and h == 0 and jp == 1:
                            b.wait("tensor", "s_s", M["gate_psum_free"])
                    P = pA if jp % 2 == 0 else pC
                    mv = b.inc("s_t", 1)
                    k13["mm1"][k] = mv

                    def f_mm1(E, S, h=h, P=P, bslot=bslot):
                        last = None
                        for gu in range(2):
                            for ft in range(NFT):
                                for ch in range(2):
                                    psl = P[:, gu * 1024 + ch * 512: gu * 1024 + ch * 512 + 512]
                                    last = E.matmul(
                                        psl,
                                        bslot[:, (gu * 8 + ft) * 128:(gu * 8 + ft) * 128 + 128],
                                        xnT[:, ft * 2048 + h * 1024 + ch * 512: ft * 2048 + h * 1024 + ch * 512 + 512],
                                        start=(ft == 0), stop=(ft == NFT - 1))
                        last.then_inc(S["s_t"], 1)

                    b.emit("tensor", f_mm1)
                    pend = (k, jp, P)
                    k13["k"] = k + 1

                emit_silu_act(*pend)
                pend = None
                kk = k13["k"]
                act_done = k13["act"][kk - 1]
                act_done2 = k13["act"][kk - 2]
                silu_done = k13["silu"][kk - 1]
                silu_done2 = k13["silu"][kk - 2]
                w2_done = cast2[15]
                w2c["last"] = w2_done

                b.wait("tensor", "s_v", act_done)
                b.wait("tensor", "s_s", w2_done)
                for tl in range(8):
                    tt = h * 8 + tl
                    P = pA if tl % 2 == 0 else pC
                    if tl >= 2:
                        b.wait("tensor", "s_v", last_resid[tl % 2])
                    else:
                        b.wait("tensor", "s_s", silu_done2 if tl % 2 == 0 else silu_done)
                        b.wait("tensor", "s_v", act_done2 if tl % 2 == 0 else act_done)
                    m2v = b.inc("s_t", 1)
                    mm2_last = m2v

                    def f_mm2(E, S, tl=tl, P=P):
                        last = None
                        for tlj in range(16):
                            for ofh in range(2):
                                psl = P[:, ofh * 512:(ofh + 1) * 512]
                                last = E.matmul(
                                    psl,
                                    actT[:, tlj * 1024 + tl * 128: tlj * 1024 + (tl + 1) * 128],
                                    w2tb[:, tlj * 1024 + ofh * 512: tlj * 1024 + ofh * 512 + 512],
                                    start=(tlj == 0), stop=(tlj == 15))
                        last.then_inc(S["s_t"], 1)

                    b.emit("tensor", f_mm2)
                    b.wait("vector", "s_t", m2v)
                    b.wait("vector", "s_v", M["combine_ready"])
                    b.wait("vector", "s_v", x_last_v[tt])
                    rv = b.inc("s_v", 1)
                    last_resid[tl % 2] = rv
                    last_resid_any = rv
                    x_last_v[tt] = rv

                    def f_rs(E, S, tt=tt, e=e, P=P):
                        E.scalar_tensor_tensor(x_tile(tt), P[:, 0:1024],
                                               combine[:, tt * 8 + e: tt * 8 + e + 1],
                                               x_tile(tt), ALU.mult, ALU.add).then_inc(S["s_v"], 1)

                    b.emit("vector", f_rs)
        return last_resid_any, mm2_last

    def emit_attn(ln2_done_s):
        war_t = b.cnt["s_t"]
        war_s = b.cnt["s_s"]
        war_v = b.cnt["s_v"]
        wt_cast = {}
        prev_cast = {}
        srcs = [("k", wkt, actT, 0), ("v", wvt, actT, 8192),
                ("q", wqt, w2tb, 0), ("o", wot, w2tb, 8192)]
        i = 0
        for (nm_, src, arena, base) in srcs:
            for ft in range(NFT):
                if i >= 2:
                    b.wait("sync", "s_s", prev_cast[i - 2])
                else:
                    b.wait("sync", "s_s", war_s)
                dv = b.inc("s_ld", 16)
                slot = w2stage[:, (i % 2) * 1024:(i % 2) * 1024 + 1024]

                def f_d(E, S, src=src, ft=ft, slot=slot):
                    E.dma_start(out=slot,
                                in_=src.rearrange("(f p) o -> f p o", p=128)[ft]).then_inc(S["s_ld"], 16)

                b.emit("sync", f_d)
                b.wait("scalar", "s_ld", dv)
                b.wait("scalar", "s_t", war_t)
                cv = b.inc("s_s", 1)
                prev_cast[i] = cv
                wt_cast[(nm_, ft)] = cv

                def f_c(E, S, arena=arena, base=base, ft=ft, slot=slot):
                    E.copy(arena[:, base + ft * 1024: base + (ft + 1) * 1024],
                           slot).then_inc(S["s_s"], 1)

                b.emit("scalar", f_c)
                i += 1
        wk_done = wt_cast[("k", NFT - 1)]
        wv_done = wt_cast[("v", NFT - 1)]
        wq_done = wt_cast[("q", NFT - 1)]
        wo_done = wt_cast[("o", NFT - 1)]

        def WK(ft, of):
            return actT[:, ft * 1024 + of * 128: ft * 1024 + of * 128 + 128]

        def WV(ft, of):
            return actT[:, 8192 + ft * 1024 + of * 128: 8192 + ft * 1024 + of * 128 + 128]

        def WQ(ft, of):
            return w2tb[:, ft * 1024 + of * 128: ft * 1024 + of * 128 + 128]

        def WO(ft, ofh):
            return w2tb[:, 8192 + ft * 1024 + ofh * 512: 8192 + ft * 1024 + ofh * 512 + 512]

        kT_of = w13stage
        vT_of = w2stage
        s_pp = xsq

        sev = {}
        ktm_ev = {}
        sev_last = 0
        ktr_last = 0
        for of in range(NFT):
            b.wait("tensor", "s_s", wk_done)
            b.wait("tensor", "s_s", ln2_done_s)
            if of > 0:
                b.wait("tensor", "s_s", ktm_ev[(of - 1, 'vev')])
            tv = b.inc("s_t", 1)

            def f_kp(E, S, of=of):
                last = None
                for ch in range(4):
                    psl = pA[:, ch * 512:(ch + 1) * 512]
                    for ft in range(NFT):
                        last = E.matmul(psl, WK(ft, of),
                                        xnT[:, ft * 2048 + ch * 512: ft * 2048 + ch * 512 + 512],
                                        start=(ft == 0), stop=(ft == NFT - 1))
                last.then_inc(S["s_t"], 1)

            b.emit("tensor", f_kp)
            b.wait("scalar", "s_t", tv)
            kevict = b.inc("s_s", 1)

            def f_ke(E, S):
                ins = None
                for ch in range(4):
                    ins = E.copy(kT_of[:, ch * 512:(ch + 1) * 512], pA[:, ch * 512:(ch + 1) * 512])
                ins.then_inc(S["s_s"], 1)

            b.emit("scalar", f_ke)
            b.wait("tensor", "s_s", wv_done)
            b.wait("tensor", "s_s", kevict)
            tv2 = b.inc("s_t", 1)

            def f_vp(E, S, of=of):
                last = None
                for ch in range(4):
                    psl = pC[:, ch * 512:(ch + 1) * 512]
                    for ft in range(NFT):
                        last = E.matmul(psl, WV(ft, of),
                                        xnT[:, ft * 2048 + ch * 512: ft * 2048 + ch * 512 + 512],
                                        start=(ft == 0), stop=(ft == NFT - 1))
                last.then_inc(S["s_t"], 1)

            b.emit("tensor", f_vp)
            b.wait("scalar", "s_t", tv2)
            b.wait("scalar", "s_s", wo_done)
            vevict = b.inc("s_s", 1)

            def f_ve(E, S):
                ins = None
                for ch in range(4):
                    ins = E.copy(vT_of[:, ch * 512:(ch + 1) * 512], pC[:, ch * 512:(ch + 1) * 512])
                ins.then_inc(S["s_s"], 1)

            b.emit("scalar", f_ve)
            tslot = [pC[:, 512:640], pC[:, 1024:1152], pC[:, 1536:1664]]
            evu = {}
            for tt in range(NTT):
                b.wait("tensor", "s_s", vevict)
                if of > 0:
                    b.wait("tensor", "s_s", sev[of - 1])
                for kv, (srcT, dstT) in enumerate(((kT_of, ktm), (vT_of, vtm))):
                    u = 2 * tt + kv
                    su = u % 3
                    if u >= 3:
                        b.wait("tensor", "s_s", evu[u - 3])
                    t_u = b.inc("s_t", 1)

                    def f_tr(E, S, tt=tt, su=su, srcT=srcT):
                        E.transpose(tslot[su], srcT[:, tt * 128:(tt + 1) * 128],
                                    ident).then_inc(S["s_t"], 1)

                    b.emit("tensor", f_tr)
                    b.wait("scalar", "s_t", t_u)
                    if tt >= 3:
                        b.wait("scalar", "s_t", ktm_ev[(of, tt - 3, 'mm')])
                    ev = b.inc("s_s", 1)
                    evu[u] = ev

                    def f_ev(E, S, tt=tt, su=su, dstT=dstT):
                        E.copy(dstT[:, (tt % 3) * 128:(tt % 3) * 128 + 128],
                               tslot[su]).then_inc(S["s_s"], 1)

                    b.emit("scalar", f_ev)
                ktm_ev[(of, tt, 'v')] = evu[2 * tt + 1]
                b.wait("tensor", "s_s", evu[2 * tt + 1])
                smv = b.inc("s_t", 1)
                ktm_ev[(of, tt, 'mm')] = smv

                def f_smm(E, S, tt=tt):
                    E.matmul(pC[:, 0:128],
                             ktm[:, (tt % 3) * 128:(tt % 3) * 128 + 128],
                             vtm[:, (tt % 3) * 128:(tt % 3) * 128 + 128],
                             start=(tt == 0), stop=(tt == NTT - 1)).then_inc(S["s_t"], 1)

                b.emit("tensor", f_smm)
            ktm_ev[(of, 'vev')] = vevict
            b.wait("scalar", "s_t", ktm_ev[(of, NTT - 1, 'mm')])
            sv = b.inc("s_s", 1)
            sev[of] = sv
            sev_last = sv

            def f_se(E, S, of=of):
                E.copy(s_sb[:, of * 128:(of + 1) * 128], pC[:, 0:128]).then_inc(S["s_s"], 1)

            b.emit("scalar", f_se)

        b.wait("gpsimd", "s_s", sev_last)
        b.wait("gpsimd", "s_v", war_v)
        b.wait("gpsimd", "s_ld", ld_after["nonce"])
        gv1 = b.inc("s_g", 16)
        gv2 = b.inc("s_g", 16)
        gv3 = b.inc("s_g", 16)
        base_g = gv1 - 16

        def f_xch(E, S, base_g=base_g):
            pid = nc.partition_id(engines=[mybir.EngineType.Pool])
            me = pid % 2
            for mv_ in range(2):
                with E.If(me == mv_):
                    E.dma_start(out=sh_s[mv_], in_=s_sb).then_inc(S["s_g"], 16)
                    E.wait_ge(S["s_g"], base_g + 16)
                    E.dma_start(out=sh_f[mv_:mv_ + 1, :], in_=sb_nonce).then_inc(S["s_g"], 16)
            E.wait_ge(S["s_g"], base_g + 32)
            nonce = E.value_load(sb_nonce)
            with E.register("pollc") as pollc:
                E.reg_mov(pollc, 16)
                for mv_ in range(2):
                    sib = 1 - mv_
                    with E.If(me == mv_):
                        def cond(sib=sib, pollc=pollc, nonce=nonce):
                            E.dma_start(out=sb_flag, in_=sh_f[sib:sib + 1, :]).then_inc(S["s_pl"], 16)
                            E.wait_ge(S["s_pl"], pollc)
                            E.reg_add(pollc, pollc, 16)
                            return E.value_load(sb_flag) != nonce

                        with E.While(cond):
                            E.nop()
                        E.dma_start(out=s_pp, in_=sh_s[sib]).then_inc(S["s_g"], 16)
                if DEBUG_TAPS:
                    E.reg_save(sb_flag, pollc)
                    E.dma_start(out=nc.dram_tensor("d_polls", [1, 1], I32,
                                                   kind="ExternalOutput").ap(),
                                in_=sb_flag).then_inc(S["s_pl"], 16)

        b.emit("gpsimd", f_xch)
        b.wait("vector", "s_g", gv3)
        v_sa = vop(lambda E: E.tensor_add(s_sb, s_sb, s_pp))
        vz = v_sa
        for hp in range(NFT):
            vz1 = vop(lambda E, hp=hp: E.memset(s_sb[0:64, hp * 128 + 64: hp * 128 + 128], 0.0),
                      deps=(v_sa,))
            vz = vop(lambda E, hp=hp: E.memset(s_sb[64:128, hp * 128: hp * 128 + 64], 0.0),
                     deps=(vz1,))
        b.wait("scalar", "s_v", vz)
        sbf_done = b.inc("s_s", 1)

        def f_sbf(E, S):
            E.copy(s_bf, s_sb).then_inc(S["s_s"], 1)

        b.emit("scalar", f_sbf)

        qb_ev = {}
        out_ev = {}
        out_mm = {}
        for hp in range(NFT):
            b.wait("tensor", "s_s", wq_done)
            if hp == 0:
                b.wait("tensor", "s_s", ktm_ev[(NFT - 1, NTT - 1, 'v')])
                b.wait("tensor", "s_s", sev[NFT - 1])
            if hp >= 1:
                b.wait("tensor", "s_s", qb_ev[hp - 1])
            if hp >= 2:
                b.wait("tensor", "s_t", out_mm[hp - 2])
            tqv = b.inc("s_t", 1)

            def f_qp(E, S, hp=hp):
                last = None
                for ch in range(4):
                    psl = pA[:, ch * 512:(ch + 1) * 512]
                    for ft in range(NFT):
                        last = E.matmul(psl, WQ(ft, hp),
                                        xnT[:, ft * 2048 + ch * 512: ft * 2048 + ch * 512 + 512],
                                        start=(ft == 0), stop=(ft == NFT - 1))
                last.then_inc(S["s_t"], 1)

            b.emit("tensor", f_qp)
            b.wait("scalar", "s_t", tqv)
            qev = b.inc("s_s", 1)
            qb_ev[hp] = qev
            qslot = w13b[:, (hp % 2) * 2048:(hp % 2) * 2048 + 2048]

            def f_qe(E, S, qslot=qslot):
                ins = None
                for ch in range(4):
                    ins = E.copy(qslot[:, ch * 512:(ch + 1) * 512], pA[:, ch * 512:(ch + 1) * 512])
                ins.then_inc(S["s_s"], 1)

            b.emit("scalar", f_qe)
            b.wait("tensor", "s_s", qev)
            b.wait("tensor", "s_s", sbf_done)
            if hp >= 1:
                b.wait("tensor", "s_s", out_ev[hp - 1])
            omv = b.inc("s_t", 1)
            out_mm[hp] = omv

            def f_om(E, S, hp=hp, qslot=qslot):
                last = None
                for ch in range(4):
                    last = E.matmul(pC[:, ch * 512:(ch + 1) * 512],
                                    s_bf[:, hp * 128:(hp + 1) * 128],
                                    qslot[:, ch * 512:(ch + 1) * 512],
                                    start=True, stop=True)
                last.then_inc(S["s_t"], 1)

            b.emit("tensor", f_om)
            b.wait("scalar", "s_t", omv)
            oev = b.inc("s_s", 1)
            out_ev[hp] = oev

            def f_oe(E, S, hp=hp):
                ins = None
                for ch in range(4):
                    ins = E.copy(actT[:, hp * 2048 + ch * 512: hp * 2048 + ch * 512 + 512],
                                 pC[:, ch * 512:(ch + 1) * 512])
                ins.then_inc(S["s_s"], 1)

            b.emit("scalar", f_oe)
        aout_done = out_ev[NFT - 1]

        attn_resid = {0: 0, 1: 0}
        wv_last = 0
        for tt in range(NTT):
            b.wait("tensor", "s_s", aout_done)
            b.wait("tensor", "s_s", wo_done)
            if tt >= 2:
                b.wait("tensor", "s_v", attn_resid[tt % 2])
            wv_last = b.inc("s_t", 1)

            def f_wo(E, S, tt=tt):
                P = pA if tt % 2 == 0 else pC
                last = None
                for ofh in range(2):
                    psl = P[:, ofh * 512:(ofh + 1) * 512]
                    for hp in range(NFT):
                        last = E.matmul(psl,
                                        actT[:, hp * 2048 + tt * 128: hp * 2048 + (tt + 1) * 128],
                                        WO(hp, ofh),
                                        start=(hp == 0), stop=(hp == NFT - 1))
                last.then_inc(S["s_t"], 1)

            b.emit("tensor", f_wo)
            b.wait("vector", "s_t", wv_last)
            b.wait("vector", "s_v", x_last_v[tt])
            rv = b.inc("s_v", 1)
            attn_resid[tt % 2] = rv
            x_last_v[tt] = rv

            def f_ar(E, S, tt=tt):
                P = pA if tt % 2 == 0 else pC
                E.tensor_tensor(x_tile(tt), x_tile(tt), P[:, 0:1024], ALU.add).then_inc(S["s_v"], 1)

            b.emit("vector", f_ar)
        return max(attn_resid[0], attn_resid[1]), wv_last

    ln1_done = emit_ln(0)
    emit_gate(ln1_done, 0)
    ffn1_resid, ffn1_mm = emit_ffn(0)

    final_resid = ffn1_resid
    if stage >= 2:
        ln2_done = emit_ln(ffn1_resid)
        attn_resid, attn_mm = emit_attn(ln2_done)
        final_resid = attn_resid
    if stage >= 3:
        ln3_done = emit_ln(attn_resid)
        emit_gate(ln3_done, attn_resid)
        ffn2_resid, ffn2_mm = emit_ffn(1)
        final_resid = ffn2_resid

    if DEBUG_TAPS:
        d_xnT = nc.dram_tensor("d_xnT", [128, NFT * 2048], BF16, kind="ExternalOutput").ap()
        d_gate = nc.dram_tensor("d_gate", [128, 128], F32, kind="ExternalOutput").ap()
        d_comb = nc.dram_tensor("d_comb", [128, 128], F32, kind="ExternalOutput").ap()
        d_act = nc.dram_tensor("d_act", [128, NJP * 1024], BF16, kind="ExternalOutput").ap()
        b.wait("gpsimd", "s_v", final_resid)
        b.wait("gpsimd", "s_s", b.cnt["s_s"])
        for (dst, src) in [(d_xnT, xnT), (d_gate, gate_sb), (d_comb, combine), (d_act, actT)]:
            gvd = b.inc("s_g", 16)

            def f_dbg(E, S, dst=dst, src=src):
                E.dma_start(out=dst, in_=src).then_inc(S["s_g"], 16)

            b.emit("gpsimd", f_dbg)

    b.wait("gpsimd", "s_v", final_resid)
    y3 = y_out.rearrange("(t p) f -> t p f", p=128)
    stv = 0
    for g in range(4):
        stv = b.inc("s_g", 16)

        def f_st(E, S, g=g):
            E.dma_start(out=y3[4 * g:4 * g + 4].rearrange("t p f -> p t f"),
                        in_=xres3[:, 4 * g:4 * g + 4, :]).then_inc(S["s_g"], 16)

        b.emit("gpsimd", f_st)

    def f_fin(E, S, stv=stv):
        E.wait_ge(S["s_g"], stv)

    b.emit("gpsimd", f_fin)

    with (
        nc.semaphore("s_ld") as s_ld,
        nc.semaphore("s_g") as s_g,
        nc.semaphore("s_t") as s_t,
        nc.semaphore("s_v") as s_v,
        nc.semaphore("s_s") as s_s,
        nc.semaphore("s_pl") as s_pl,
        nc.Block() as block,
    ):
        S = {"s_ld": s_ld, "s_g": s_g, "s_t": s_t, "s_v": s_v, "s_s": s_s, "s_pl": s_pl}

        @block.sync
        def _(eng):
            for f in b.ops["sync"]:
                f(eng, S)

        @block.tensor
        def _(eng):
            for f in b.ops["tensor"]:
                f(eng, S)

        @block.scalar
        def _(eng):
            for f in b.ops["scalar"]:
                f(eng, S)

        @block.vector
        def _(eng):
            for f in b.ops["vector"]:
                f(eng, S)

        @block.gpsimd
        def _(eng):
            for f in b.ops["gpsimd"]:
                f(eng, S)

    return nc


def _prep_weights(Wq, Wk, Wv, Wo, gate_w, w13, w2):
    w13t = np.ascontiguousarray(
        w13.reshape(8, 2, 16, 128, 8, 128).transpose(0, 2, 5, 1, 4, 3))
    w2t = np.ascontiguousarray(w2.transpose(0, 2, 1).reshape(8, 16, 128, 1024))
    return {
        "w13t": w13t,
        "w2t": w2t,
        "wqt": np.ascontiguousarray(Wq.T),
        "wkt": np.ascontiguousarray(Wk.T),
        "wvt": np.ascontiguousarray(Wv.T),
        "wot": np.ascontiguousarray(Wo.T),
        "gwt": np.ascontiguousarray(gate_w.T),
    }


def kernel(L0_attn_h, L0_attn_key_pe, L0_attn_Wq, L0_attn_Wk, L0_attn_Wv,
           L0_attn_Wo, L0_ffn_gate_w, L0_ffn_w13_weight, L0_ffn_w2_weight):
    h = np.asarray(L0_attn_h, dtype=np.float32)
    _ = np.asarray(L0_attn_key_pe)
    W = _prep_weights(np.asarray(L0_attn_Wq, np.float32), np.asarray(L0_attn_Wk, np.float32),
                      np.asarray(L0_attn_Wv, np.float32), np.asarray(L0_attn_Wo, np.float32),
                      np.asarray(L0_ffn_gate_w, np.float32),
                      np.asarray(L0_ffn_w13_weight, np.float32),
                      np.asarray(L0_ffn_w2_weight, np.float32))
    nonce = np.random.randint(1, 2 ** 30, size=(1, 1)).astype(np.int32)
    in_maps = []
    for c in range(8):
        bidx, half = c // 2, c % 2
        xs = np.ascontiguousarray(h[bidx, half * T:(half + 1) * T, :])
        m = {"x_in": xs, "nonce": nonce}
        m.update(W)
        in_maps.append(m)
    nc = build_nc(STAGE)
    res = run_bass_kernel_spmd(nc, in_maps, core_ids=list(range(8)))
    global LAST_RES
    LAST_RES = res
    out = np.empty((4, 4096, 1024), dtype=np.float32)
    for c in range(8):
        bidx, half = c // 2, c % 2
        out[bidx, half * T:(half + 1) * T, :] = res.results[c]["y_out"]
    return out


# version 08
# speedup vs baseline: 1.4250x; 1.4250x over previous
import numpy as np
import ml_dtypes
import concourse.bass as bass
import concourse.mybir as mybir
from concourse.bass_utils import run_bass_kernel_spmd

F32 = mybir.dt.float32
BF16 = mybir.dt.bfloat16
I32 = mybir.dt.int32
AF = mybir.ActivationFunctionType
ALU = mybir.AluOpType
AX = mybir.AxisListType

T = 2048
NTT = 16
NFT = 8
NE = 8
NJP = 16
NH = 2
EPS = 1e-5

STAGE = 3
DEBUG_TAPS = False
LAST_RES = None

ENGS = ("tensor", "vector", "scalar", "sync", "gpsimd")
SEMS = ("s_ld", "s_da", "s_db", "s_ga", "s_gb", "s_sa", "s_sb", "s_x", "s_st", "s_t", "s_v", "s_s")


class B:

    def __init__(self):
        self.ops = {e: [] for e in ENGS}
        self.cnt = {s: 0 for s in SEMS}
        self.lastw = {}

    def emit(self, eng, fn):
        self.ops[eng].append(fn)

    def wait(self, eng, sem, val):
        if val <= 0:
            return
        k = (eng, sem)
        if self.lastw.get(k, 0) >= val:
            return
        self.lastw[k] = val

        def f(E, S, sem=sem, val=val):
            E.wait_ge(S[sem], val)

        self.ops[eng].append(f)

    def inc(self, sem, by):
        self.cnt[sem] += by
        return self.cnt[sem]


def build_nc(stage=STAGE):
    nc = bass.Bass("TRN2", target_bir_lowering=False)

    x_in = nc.dram_tensor("x_in", [T, 1024], F32, kind="ExternalInput").ap()
    w13t = nc.dram_tensor("w13t", [NE, NJP, 128, 2, 8, 128], F32, kind="ExternalInput").ap()
    w2t = nc.dram_tensor("w2t", [NE, 16, 128, 1024], F32, kind="ExternalInput").ap()
    wqt = nc.dram_tensor("wqt", [1024, 1024], F32, kind="ExternalInput").ap()
    wkt = nc.dram_tensor("wkt", [1024, 1024], F32, kind="ExternalInput").ap()
    wvt = nc.dram_tensor("wvt", [1024, 1024], F32, kind="ExternalInput").ap()
    wot = nc.dram_tensor("wot", [1024, 1024], F32, kind="ExternalInput").ap()
    gwt = nc.dram_tensor("gwt", [1024, 8], F32, kind="ExternalInput").ap()
    nonce_in = nc.dram_tensor("nonce", [1, 1], I32, kind="ExternalInput").ap()
    y_out = nc.dram_tensor("y_out", [T, 1024], F32, kind="ExternalOutput").ap()
    ident_d = nc.inline_tensor(np.eye(128, dtype=np.float32), "ident_c").ap()

    sh_s = nc.dram_tensor("sh_s", [2, 128, 1024], F32, kind="Internal", addr_space="Shared").ap()
    sh_f = nc.dram_tensor("sh_f", [2, 1], I32, kind="Internal", addr_space="Shared").ap()
    xnb_dram = nc.dram_tensor("xnb_dram", [T, 1024], BF16, kind="Internal").ap()

    O = 20992

    def at(name, shape, dtype, off):
        return nc.alloc_sbuf_tensor_at(name, shape, dtype, offset=O + off).ap()

    x_res = at("x_res", [128, NTT * 1024], F32, 0)
    xnT = at("xnT", [128, NFT * 2048], BF16, 65536)
    actT = at("actT", [128, NJP * 1024], BF16, 98304)
    w2tb = at("w2tb", [128, 16 * 1024], BF16, 131072)
    w13stage = at("w13stage", [128, 2048], F32, 163840)
    w2stage = at("w2stage", [128, 2 * 1024], F32, 172032)
    w13b = at("w13b", [128, 2 * 2048], BF16, 180224)
    ident = at("ident", [128, 128], F32, 188416)
    xsq = at("xsq", [128, 1024], F32, 188928)
    gstat = at("gstat", [128, 8 * 16], F32, 188928)
    gbuf = at("gbuf", [128, 4 * 128], F32, 189440)
    gtmp4 = at("gtmp4", [128, 3 * 64], F32, 191488)
    xn_tok = at("xn_tok", [128, 1024], F32, 193024)
    s_sb = at("s_sb", [128, 1024], F32, 193024)
    silu_t = at("silu_t", [128, 1024], BF16, 197120)
    s_bf = at("s_bf", [128, 1024], BF16, 197120)
    gate_sb = at("gate_sb", [128, 128], F32, 199168)
    combine = at("combine", [128, 128], F32, 200192)
    dblk = at("dblk", [128, 2048], BF16, 202240)
    cbst0 = at("cbst0", [128, 1024], BF16, 163840)
    cbst1 = at("cbst1", [128, 1024], BF16, 165888)
    lnstat = at("lnstat", [128, 128], F32, 201216)
    gwstage = at("gwstage", [128, 64], F32, 201728)
    gwtb = at("gwtb", [128, 64], BF16, 201984)
    sb_nonce = at("sb_nonce", [1, 1], I32, 202112)
    sb_flag = at("sb_flag", [1, 1], I32, 202176)

    pA = nc.alloc_psum_tensor("pA", [128, 2048], F32).ap()
    pC = nc.alloc_psum_tensor("pC", [128, 2048], F32).ap()

    b = B()
    M = {}
    x_last_v = {tt: 0 for tt in range(NTT)}

    ld_after = {}

    def sdma(out_ap, in_ap, tag=None):
        v = b.inc("s_ld", 16)

        def f(E, S, out_ap=out_ap, in_ap=in_ap):
            E.dma_start(out=out_ap, in_=in_ap).then_inc(S["s_ld"], 16)

        b.emit("sync", f)
        if tag is not None:
            ld_after[tag] = v
        return v

    x_in3 = x_in.rearrange("(t p) f -> t p f", p=128)
    xres3 = x_res.rearrange("p (t f) -> p t f", f=1024)
    for g in range(4):
        sdma(xres3[:, 4 * g:4 * g + 4, :],
             x_in3[4 * g:4 * g + 4].rearrange("t p f -> p t f"), tag=("x", g))
    sdma(ident, ident_d, tag="ident")
    sdma(gwstage.rearrange("p (f e) -> p f e", e=8),
         gwt.rearrange("(f p) e -> p f e", p=128), tag="gw")
    sdma(sb_nonce, nonce_in, tag="nonce")

    b.wait("scalar", "s_ld", ld_after["nonce"])
    M["gw_cast"] = b.inc("s_s", 1)

    def f_gw(E, S):
        E.copy(gwtb, gwstage).then_inc(S["s_s"], 1)

    b.emit("scalar", f_gw)

    def x_tile(tt):
        return x_res[:, tt * 1024:(tt + 1) * 1024]

    def vop(build, deps=()):
        for d in deps:
            b.wait("vector", "s_v", d)
        v = b.inc("s_v", 1)

        def f(E, S, build=build):
            build(E).then_inc(S["s_v"], 1)

        b.emit("vector", f)
        return v

    def emit_ln(x_ready, to_dram=False):
        if not isinstance(x_ready, dict):
            x_ready = {tt: x_ready for tt in range(NTT)}
        x_ready_all = max(x_ready.values())
        evict_val = {}
        tr_val = {}
        b.wait("vector", "s_ld", ld_after["nonce"])
        b.wait("scalar", "s_ld", ld_after["nonce"])
        b.wait("tensor", "s_ld", ld_after["nonce"])
        s_sums = {}
        for tt in range(NTT):
            b.wait("scalar", "s_v", x_ready[tt])
            sva = b.inc("s_s", 1)

            def f_sq(E, S, tt=tt):
                E.activation(xsq, x_tile(tt), AF.Square,
                             accum_out=lnstat[:, 16 + tt:16 + tt + 1]).then_inc(S["s_s"], 1)

            b.emit("scalar", f_sq)
            b.wait("scalar", "s_s", sva)
            sv = b.inc("s_s", 1)
            s_sums[tt] = sv

            def f_cp(E, S, tt=tt):
                E.activation(xsq, x_tile(tt), AF.Copy,
                             accum_out=lnstat[:, tt:tt + 1]).then_inc(S["s_s"], 1)

            b.emit("scalar", f_cp)
            b.wait("scalar", "s_s", sv)
        b.wait("vector", "s_s", s_sums[NTT - 1])
        Lc = lnstat
        v_u = vop(lambda E: E.tensor_tensor(Lc[:, 32:48], Lc[:, 0:16], Lc[:, 0:16], ALU.mult))
        v_w = vop(lambda E: E.scalar_tensor_tensor(Lc[:, 48:64], Lc[:, 32:48], -1.0 / 1024.0,
                                                   Lc[:, 16:32], ALU.mult, ALU.add), deps=(v_u,))
        v_w2 = vop(lambda E: E.tensor_scalar(Lc[:, 64:80], Lc[:, 48:64], 1.0 / 1024.0, EPS,
                                             ALU.mult, ALU.add), deps=(v_w,))
        b.wait("scalar", "s_v", v_w2)
        sv2 = b.inc("s_s", 1)

        def f_sqrt(E, S):
            E.activation(Lc[:, 80:96], Lc[:, 64:80], AF.Sqrt).then_inc(S["s_s"], 1)

        b.emit("scalar", f_sqrt)
        b.wait("vector", "s_s", sv2)
        v_r = vop(lambda E: E.reciprocal(Lc[:, 96:112], Lc[:, 80:96]))
        v_n = vop(lambda E: E.tensor_scalar(Lc[:, 112:128], Lc[:, 0:16], -1.0 / 1024.0,
                                            None, ALU.mult))
        dram_v = {}
        xslots = [xn_tok, w2stage[:, 0:1024]]
        tps_ = [pA[:, 0:128], pA[:, 512:640], pA[:, 1024:1152], pA[:, 1536:1664]]
        cast_v = {}
        for tt in range(NTT):
            xslot = xslots[tt % 2]
            if tt >= 2:
                b.wait("vector", "s_t", tr_val[(tt - 2, NFT - 1)])
            if to_dram and tt >= 2:
                b.wait("vector", "s_sa" if tt % 2 == 0 else "s_sb", dram_v[tt - 2])
                b.wait("vector", "s_s", cast_v[tt - 2])
            av = vop(lambda E, tt=tt, xslot=xslot: E.scalar_tensor_tensor(
                xslot, x_tile(tt), Lc[:, 112 + tt:112 + tt + 1],
                Lc[:, 96 + tt:96 + tt + 1].broadcast_to([128, 1024]),
                ALU.add, ALU.mult),
                deps=(v_n, v_r, x_ready[tt]))
            if to_dram:
                b.wait("scalar", "s_v", av)
                if tt >= 2:
                    b.wait("scalar", "s_sa" if tt % 2 == 0 else "s_sb", dram_v[tt - 2])
                scv = b.inc("s_s", 1)
                cast_v[tt] = scv
                cslot = [silu_t, s_bf][tt % 2]

                def f_cast(E, S, xslot=xslot, cslot=cslot):
                    E.copy(cslot, xslot).then_inc(S["s_s"], 1)

                b.emit("scalar", f_cast)
                b.wait("sync", "s_s", scv)
                ssem = "s_sa" if tt % 2 == 0 else "s_sb"
                dram_v[tt] = b.inc(ssem, 16)

                def f_xd(E, S, tt=tt, cslot=cslot, ssem=ssem):
                    E.dma_start(out=xnb_dram[tt * 128:(tt + 1) * 128, :],
                                in_=cslot).then_inc(S[ssem], 16)

                b.emit("sync", f_xd)
                if tt == NTT - 1:
                    M["xnb_ready"] = (b.cnt["s_sa"], b.cnt["s_sb"])
            for ft in range(NFT):
                b.wait("tensor", "s_v", av)
                idx = tt * NFT + ft
                if idx >= 4:
                    pidx = idx - 4
                    b.wait("tensor", "s_s", evict_val[(pidx // NFT, pidx % NFT)])
                tv = b.inc("s_t", 1)
                tr_val[(tt, ft)] = tv
                ps = tps_[idx % 4]

                def f_tr(E, S, ps=ps, ft=ft, xslot=xslot):
                    E.transpose(ps, xslot[:, ft * 128:(ft + 1) * 128], ident).then_inc(S["s_t"], 1)

                b.emit("tensor", f_tr)
                b.wait("scalar", "s_t", tv)
                ev = b.inc("s_s", 1)
                evict_val[(tt, ft)] = ev

                def f_ev(E, S, ps=ps, ft=ft, tt=tt):
                    E.copy(xnT[:, ft * 2048 + tt * 128: ft * 2048 + (tt + 1) * 128],
                           ps).then_inc(S["s_s"], 1)

                b.emit("scalar", f_ev)
        return b.cnt["s_s"]

    def emit_gate(ln_done_s, psum_free_v):
        b.wait("tensor", "s_s", ln_done_s)
        b.wait("tensor", "s_s", M["gw_cast"])
        b.wait("tensor", "s_v", psum_free_v)
        tv = b.inc("s_t", 1)

        def f_gmm(E, S):
            last = None
            for tt in range(NTT):
                for ft in range(NFT):
                    last = E.matmul(pC[:, tt * 8:(tt + 1) * 8],
                                    xnT[:, ft * 2048 + tt * 128: ft * 2048 + (tt + 1) * 128],
                                    gwtb[:, ft * 8:(ft + 1) * 8],
                                    start=(ft == 0), stop=(ft == NFT - 1))
            last.then_inc(S["s_t"], 1)

        b.emit("tensor", f_gmm)
        b.wait("scalar", "s_t", tv)
        gev = b.inc("s_s", 1)

        def f_gev(E, S):
            E.copy(gate_sb, pC[:, 0:128]).then_inc(S["s_s"], 1)

        b.emit("scalar", f_gev)
        M["gate_psum_free"] = gev

        gs3 = gate_sb.rearrange("p (t e) -> p t e", e=8)
        el = gbuf[:, 0:128]
        el3 = el.rearrange("p (t e) -> p t e", e=8)
        masked = gbuf[:, 128:256]
        m3 = masked.rearrange("p (t e) -> p t e", e=8)
        mk2 = gbuf[:, 256:384]
        mk23 = mk2.rearrange("p (t e) -> p t e", e=8)
        wsel = gbuf[:, 384:512]
        w3 = wsel.rearrange("p (t e) -> p t e", e=8)
        g4 = gtmp4[:, 0:64].rearrange("p (t e) -> p t e", e=4)
        eqg = gtmp4[:, 64:128].rearrange("p (t e) -> p t e", e=4)
        g4b = gtmp4[:, 128:192].rearrange("p (t e) -> p t e", e=4)
        stt_ = gstat

        def bc(col, n=8):
            return stt_[:, col * 16:(col + 1) * 16].unsqueeze(2).broadcast_to([128, 16, n])

        b.wait("vector", "s_s", gev)
        v1 = vop(lambda E: E.reduce_max(stt_[:, 0:16], gs3, axis=AX.X))
        v2 = vop(lambda E: E.tensor_tensor(m3, gs3, bc(0), ALU.subtract), deps=(v1,))
        b.wait("scalar", "s_v", v2)
        s1 = b.inc("s_s", 1)

        def f_exp(E, S):
            E.activation(el, masked, AF.Exp).then_inc(S["s_s"], 1)

        b.emit("scalar", f_exp)
        b.wait("vector", "s_s", s1)
        v3a = vop(lambda E: E.tensor_tensor(g4, el3[:, :, 0:8:2], el3[:, :, 1:8:2], ALU.max))
        v3 = vop(lambda E: E.reduce_max(stt_[:, 16:32], g4, axis=AX.X), deps=(v3a,))
        v4a = vop(lambda E: E.tensor_tensor(eqg, g4, bc(1, 4), ALU.is_equal), deps=(v3,))
        v4b = vop(lambda E: E.scalar_tensor_tensor(g4b, eqg, -1e30, g4, ALU.mult, ALU.add),
                  deps=(v4a,))
        v4 = vop(lambda E: E.reduce_max(stt_[:, 32:48], g4b, axis=AX.X), deps=(v4b,))
        v5a = vop(lambda E: E.tensor_tensor(eqg, g4, bc(2, 4), ALU.is_ge), deps=(v4,))
        v5b = vop(lambda E: E.tensor_tensor(m3[:, :, 0:8:2], el3[:, :, 0:8:2], eqg, ALU.mult),
                  deps=(v5a,))
        v5c = vop(lambda E: E.tensor_tensor(m3[:, :, 1:8:2], el3[:, :, 1:8:2], eqg, ALU.mult),
                  deps=(v5a,))
        v5 = vop(lambda E: E.reduce_max(stt_[:, 48:64], m3, axis=AX.X), deps=(v5b, v5c))
        v6a = vop(lambda E: E.tensor_tensor(mk23, m3, bc(3), ALU.is_equal), deps=(v5,))
        v6b = vop(lambda E: E.scalar_tensor_tensor(mk23, mk23, -1e30, m3, ALU.mult, ALU.add),
                  deps=(v6a,))
        v6 = vop(lambda E: E.reduce_max(stt_[:, 64:80], mk23, axis=AX.X), deps=(v6b,))
        v7a = vop(lambda E: E.tensor_tensor(mk23, m3, bc(4), ALU.is_ge), deps=(v6,))
        v7b = vop(lambda E: E.tensor_tensor(w3, m3, mk23, ALU.mult), deps=(v7a,))
        v7 = vop(lambda E: E.reduce_sum(stt_[:, 80:96], w3, axis=AX.X), deps=(v7b,))
        v8a = vop(lambda E: E.tensor_scalar(stt_[:, 96:112], stt_[:, 80:96], 2.0, None, ALU.mult),
                  deps=(v7,))
        v8 = vop(lambda E: E.reciprocal(stt_[:, 112:128], stt_[:, 96:112]), deps=(v8a,))
        v9 = vop(lambda E: E.tensor_tensor(combine.rearrange("p (t e) -> p t e", e=8), w3,
                                           bc(7), ALU.mult), deps=(v8,))
        M["combine_ready"] = v9

    k13 = {"cast": {}, "mm1": {}, "silu": {}, "act": {}, "k": 0}
    w2c = {}

    def emit_ffn(call):
        war_t = b.cnt["s_t"]
        war_s = b.cnt["s_s"]
        war_v = b.cnt["s_v"]
        k_start = k13["k"]
        last_resid = {0: war_v, 1: war_v}
        last_resid_any = war_v
        mm2_last = war_t
        w2_prev_cast = w2c.get("last", 0)

        def emit_silu_act(k, jp, P):
            b.wait("scalar", "s_t", k13["mm1"][k])
            if k - k_start >= 1:
                b.wait("scalar", "s_v", k13["act"][k - 1])
            lv = b.inc("s_s", 1)
            k13["silu"][k] = lv

            def f_si(E, S, P=P):
                E.activation(silu_t, P[:, 0:1024], AF.Silu).then_inc(S["s_s"], 1)

            b.emit("scalar", f_si)
            b.wait("vector", "s_s", lv)
            av = b.inc("s_v", 1)
            k13["act"][k] = av

            def f_ac(E, S, jp=jp, P=P):
                E.tensor_tensor(actT[:, jp * 1024:(jp + 1) * 1024], silu_t,
                                P[:, 1024:2048], ALU.mult).then_inc(S["s_v"], 1)

            b.emit("vector", f_ac)

        for e in range(NE):
            cast2 = {}
            pend = None
            for h in range(NH):
                for jp in range(NJP):
                    k = k13["k"]
                    if k - k_start >= 1:
                        b.wait("sync", "s_s", k13["cast"][k - 1])
                    else:
                        b.wait("sync", "s_t", war_t)
                        b.wait("sync", "s_s", war_s)
                    dv = b.inc("s_ld", 16)
                    bslot = w13b[:, (k % 2) * 2048:(k % 2) * 2048 + 2048]

                    def f_wd(E, S, e=e, jp=jp):
                        E.dma_start(out=w13stage,
                                    in_=w13t[e, jp].rearrange("p g f m -> p (g f m)")).then_inc(S["s_ld"], 16)

                    b.emit("sync", f_wd)
                    b.wait("scalar", "s_ld", dv)
                    if k - k_start >= 2:
                        b.wait("scalar", "s_t", k13["mm1"][k - 2])
                    else:
                        b.wait("scalar", "s_t", war_t)
                    cv = b.inc("s_s", 1)
                    k13["cast"][k] = cv

                    def f_wc(E, S, bslot=bslot):
                        E.copy(bslot, w13stage).then_inc(S["s_s"], 1)

                    b.emit("scalar", f_wc)
                    if h == 0:
                        j = jp
                        if j >= 2:
                            b.wait("gpsimd", "s_s", cast2[j - 2])
                        else:
                            b.wait("gpsimd", "s_s", w2_prev_cast if e == 0 else w2c[(call, e - 1, 15)])
                            if e == 0:
                                b.wait("gpsimd", "s_t", war_t)
                                if call > 0:
                                    b.wait("gpsimd", "s_s", war_s)
                        gsem = "s_ga" if j % 2 == 0 else "s_gb"
                        gv = b.inc(gsem, 16)
                        slot2 = w2stage[:, (j % 2) * 1024:(j % 2) * 1024 + 1024]

                        def f_d(E, S, e=e, j=j, slot2=slot2, gsem=gsem):
                            E.dma_start(out=slot2, in_=w2t[e, j]).then_inc(S[gsem], 16)

                        b.emit("gpsimd", f_d)
                        b.wait("scalar", gsem, gv)
                        b.wait("scalar", "s_t", mm2_last)
                        c2 = b.inc("s_s", 1)
                        cast2[j] = c2
                        w2c[(call, e, j)] = c2

                        def f_c(E, S, j=j, slot2=slot2):
                            E.copy(w2tb[:, j * 1024:(j + 1) * 1024], slot2).then_inc(S["s_s"], 1)

                        b.emit("scalar", f_c)
                    if pend is not None:
                        emit_silu_act(*pend)
                    b.wait("tensor", "s_s", cv)
                    if k - k_start >= 2:
                        b.wait("tensor", "s_s", k13["silu"][k - 2])
                        b.wait("tensor", "s_v", k13["act"][k - 2])
                    if jp <= 1:
                        b.wait("tensor", "s_v", last_resid_any)
                        if e == 0 and h == 0 and jp == 1:
                            b.wait("tensor", "s_s", M["gate_psum_free"])
                    P = pA if jp % 2 == 0 else pC
                    mv = b.inc("s_t", 1)
                    k13["mm1"][k] = mv

                    def f_mm1(E, S, h=h, P=P, bslot=bslot):
                        last = None
                        for gu in range(2):
                            for ft in range(NFT):
                                for ch in range(2):
                                    psl = P[:, gu * 1024 + ch * 512: gu * 1024 + ch * 512 + 512]
                                    last = E.matmul(
                                        psl,
                                        bslot[:, (gu * 8 + ft) * 128:(gu * 8 + ft) * 128 + 128],
                                        xnT[:, ft * 2048 + h * 1024 + ch * 512: ft * 2048 + h * 1024 + ch * 512 + 512],
                                        start=(ft == 0), stop=(ft == NFT - 1))
                        last.then_inc(S["s_t"], 1)

                    b.emit("tensor", f_mm1)
                    pend = (k, jp, P)
                    k13["k"] = k + 1

                emit_silu_act(*pend)
                pend = None
                kk = k13["k"]
                act_done = k13["act"][kk - 1]
                act_done2 = k13["act"][kk - 2]
                silu_done = k13["silu"][kk - 1]
                silu_done2 = k13["silu"][kk - 2]
                w2_done = cast2[15]
                w2c["last"] = w2_done

                b.wait("tensor", "s_v", act_done)
                b.wait("tensor", "s_s", w2_done)
                for tl in range(8):
                    tt = h * 8 + tl
                    P = pA if tl % 2 == 0 else pC
                    if tl >= 2:
                        b.wait("tensor", "s_v", last_resid[tl % 2])
                    else:
                        b.wait("tensor", "s_s", silu_done2 if tl % 2 == 0 else silu_done)
                        b.wait("tensor", "s_v", act_done2 if tl % 2 == 0 else act_done)
                    m2v = b.inc("s_t", 1)
                    mm2_last = m2v

                    def f_mm2(E, S, tl=tl, P=P):
                        last = None
                        for tlj in range(16):
                            for ofh in range(2):
                                psl = P[:, ofh * 512:(ofh + 1) * 512]
                                last = E.matmul(
                                    psl,
                                    actT[:, tlj * 1024 + tl * 128: tlj * 1024 + (tl + 1) * 128],
                                    w2tb[:, tlj * 1024 + ofh * 512: tlj * 1024 + ofh * 512 + 512],
                                    start=(tlj == 0), stop=(tlj == 15))
                        last.then_inc(S["s_t"], 1)

                    b.emit("tensor", f_mm2)
                    b.wait("vector", "s_t", m2v)
                    b.wait("vector", "s_v", M["combine_ready"])
                    b.wait("vector", "s_v", x_last_v[tt])
                    rv = b.inc("s_v", 1)
                    last_resid[tl % 2] = rv
                    last_resid_any = rv
                    x_last_v[tt] = rv

                    def f_rs(E, S, tt=tt, e=e, P=P):
                        E.scalar_tensor_tensor(x_tile(tt), P[:, 0:1024],
                                               combine[:, tt * 8 + e: tt * 8 + e + 1],
                                               x_tile(tt), ALU.mult, ALU.add).then_inc(S["s_v"], 1)

                    b.emit("vector", f_rs)
        return last_resid_any, mm2_last

    def emit_attn(ln2_done_s):
        war_t = b.cnt["s_t"]
        war_s = b.cnt["s_s"]
        war_v = b.cnt["s_v"]
        def stream_weights(srcs, i0, war_tv):
            prev_cast = {}
            outv = {}
            i = i0
            for (nm_, src, base) in srcs:
                for ft in range(NFT):
                    if i - i0 >= 2:
                        b.wait("sync", "s_s", prev_cast[i - 2])
                    else:
                        b.wait("sync", "s_s", war_s)
                        b.wait("sync", "s_t", war_tv)
                    gsem = "s_ga" if i % 2 == 0 else "s_gb"
                    dv = b.inc(gsem, 16)
                    slot = w2stage[:, (i % 2) * 1024:(i % 2) * 1024 + 1024]

                    def f_d(E, S, src=src, ft=ft, slot=slot, gsem=gsem):
                        E.dma_start(out=slot,
                                    in_=src.rearrange("(f p) o -> f p o", p=128)[ft]).then_inc(S[gsem], 16)

                    b.emit("sync", f_d)
                    b.wait("scalar", gsem, dv)
                    b.wait("scalar", "s_t", war_tv)
                    cv = b.inc("s_s", 1)
                    prev_cast[i] = cv
                    outv[(nm_, ft)] = cv

                    def f_c(E, S, base=base, ft=ft, slot=slot):
                        E.copy(w2tb[:, base + ft * 1024: base + (ft + 1) * 1024],
                               slot).then_inc(S["s_s"], 1)

                    b.emit("scalar", f_c)
                    i += 1
            return outv, i

        wt1, wi = stream_weights([("k", wkt, 0), ("v", wvt, 8192)], 0, war_t)
        wk_done = wt1[("k", NFT - 1)]
        wv_done = wt1[("v", NFT - 1)]

        def WQ(ft, of):
            return w2tb[:, ft * 1024 + of * 128: ft * 1024 + of * 128 + 128]

        def WO(ft, ofh):
            return w2tb[:, 8192 + ft * 1024 + ofh * 512: 8192 + ft * 1024 + ofh * 512 + 512]

        s_pp = xsq
        cbst = [cbst0, cbst1]

        def C_sb(ftile):
            return actT[:, ftile * 1024:(ftile + 1) * 1024]

        cmm = {}
        cev = {}
        b.wait("tensor", "s_v", war_v)
        b.wait("tensor", "s_s", ln2_done_s)
        for ps_ in range(2):
            for tt in range(NTT):
                if (ps_, tt - 2) in cmm:
                    b.wait("sync", "s_t", cmm[(ps_, tt - 2)])
                elif ps_ > 0:
                    b.wait("sync", "s_t", cmm[(0, NTT - 1)])
                b.wait("sync", "s_sa", M["xnb_ready"][0])
                b.wait("sync", "s_sb", M["xnb_ready"][1])
                dsem = "s_da" if tt % 2 == 0 else "s_db"
                dv = b.inc(dsem, 16)

                def f_cd(E, S, tt=tt, sl=tt % 2, dsem=dsem):
                    E.dma_start(out=cbst[sl],
                                in_=xnb_dram[tt * 128:(tt + 1) * 128, :]).then_inc(S[dsem], 16)

                b.emit("sync", f_cd)
                b.wait("tensor", dsem, dv)
                if ps_ > 0 and tt == 0:
                    b.wait("tensor", "s_s", cev[(0, 3)])
                cv = b.inc("s_t", 1)
                cmm[(ps_, tt)] = cv

                def f_cm(E, S, ps_=ps_, tt=tt, sl=tt % 2):
                    last = None
                    for m in range(4):
                        P = pA if m < 2 else pC
                        base = (m % 2) * 1024
                        for nh in range(2):
                            last = E.matmul(
                                P[:, base + nh * 512: base + nh * 512 + 512],
                                cbst[sl][:, (ps_ * 4 + m) * 128:(ps_ * 4 + m) * 128 + 128],
                                cbst[sl][:, nh * 512: nh * 512 + 512],
                                start=(tt == 0), stop=(tt == NTT - 1))
                    last.then_inc(S["s_t"], 1)

                b.emit("tensor", f_cm)
            b.wait("scalar", "s_t", cmm[(ps_, NTT - 1)])
            for m in range(4):
                ev = b.inc("s_s", 1)
                cev[(ps_, m)] = ev

                def f_ce(E, S, ps_=ps_, m=m):
                    P = pA if m < 2 else pC
                    base = (m % 2) * 1024
                    E.copy(C_sb(ps_ * 4 + m), P[:, base:base + 1024]).then_inc(S["s_s"], 1)

                b.emit("scalar", f_ce)
        c_done = cev[(1, 3)]

        sev = {}
        sev_last = 0
        dev = {}
        last_smm = 0
        for hpp in range(4):
            dmm = {}
            for m in range(NFT):
                b.wait("tensor", "s_s", c_done)
                b.wait("tensor", "s_s", wv_done)
                if m >= 4:
                    b.wait("tensor", "s_s", dev[(hpp, m - 4)])
                elif hpp > 0:
                    b.wait("tensor", "s_s", dev[(hpp - 1, m + 4)])
                dm = b.inc("s_t", 1)
                dmm[m] = dm
                psl = pA[:, (m % 4) * 512:(m % 4) * 512 + 256]

                def f_dm(E, S, hpp=hpp, m=m, psl=psl):
                    last = None
                    for fp in range(NFT):
                        last = E.matmul(psl,
                                        C_sb(fp)[:, m * 128:(m + 1) * 128],
                                        w2tb[:, 8192 + fp * 1024 + hpp * 256: 8192 + fp * 1024 + hpp * 256 + 256],
                                        start=(fp == 0), stop=(fp == NFT - 1))
                    last.then_inc(S["s_t"], 1)

                b.emit("tensor", f_dm)
                b.wait("scalar", "s_t", dm)
                ev = b.inc("s_s", 1)
                dev[(hpp, m)] = ev

                def f_de(E, S, m=m, psl=psl):
                    E.copy(dblk[:, m * 256:(m + 1) * 256], psl).then_inc(S["s_s"], 1)

                b.emit("scalar", f_de)
            for hk in range(2):
                hp = hpp * 2 + hk
                b.wait("tensor", "s_s", dev[(hpp, NFT - 1)])
                b.wait("tensor", "s_s", wk_done)
                if hp > 0:
                    b.wait("tensor", "s_s", sev[hp - 1])
                sm = b.inc("s_t", 1)
                last_smm = sm

                def f_sm(E, S, hp=hp, hk=hk):
                    last = None
                    for ft in range(NFT):
                        last = E.matmul(pC[:, 0:128],
                                        w2tb[:, ft * 1024 + hp * 128: ft * 1024 + hp * 128 + 128],
                                        dblk[:, ft * 256 + hk * 128: ft * 256 + hk * 128 + 128],
                                        start=(ft == 0), stop=(ft == NFT - 1))
                    last.then_inc(S["s_t"], 1)

                b.emit("tensor", f_sm)
                b.wait("scalar", "s_t", sm)
                sv = b.inc("s_s", 1)
                sev[hp] = sv
                sev_last = sv

                def f_se(E, S, hp=hp):
                    E.copy(s_sb[:, hp * 128:(hp + 1) * 128], pC[:, 0:128]).then_inc(S["s_s"], 1)

                b.emit("scalar", f_se)

        wt2, _ = stream_weights([("q", wqt, 0), ("o", wot, 8192)], wi, last_smm)
        wq_done = wt2[("q", NFT - 1)]
        wo_done = wt2[("o", NFT - 1)]

        b.wait("gpsimd", "s_s", sev_last)
        b.wait("gpsimd", "s_v", war_v)
        b.wait("gpsimd", "s_ld", ld_after["nonce"])
        gv1 = b.inc("s_x", 16)
        gv2 = b.inc("s_x", 16)
        gv3 = b.inc("s_x", 16)
        base_g = gv1 - 16

        def f_xch(E, S, base_g=base_g):
            pid = nc.partition_id(engines=[mybir.EngineType.Pool])
            me = pid % 2
            for mv_ in range(2):
                with E.If(me == mv_):
                    E.dma_start(out=sh_s[mv_], in_=s_sb).then_inc(S["s_x"], 16)
                    E.wait_ge(S["s_x"], base_g + 16)
                    E.dma_start(out=sh_f[mv_:mv_ + 1, :], in_=sb_nonce).then_inc(S["s_x"], 16)
            E.wait_ge(S["s_x"], base_g + 32)
            nonce = E.value_load(sb_nonce)
            with E.register("pollc") as pollc:
                E.reg_mov(pollc, 16)
                for mv_ in range(2):
                    sib = 1 - mv_
                    with E.If(me == mv_):
                        def cond(sib=sib, pollc=pollc, nonce=nonce):
                            E.dma_start(out=sb_flag, in_=sh_f[sib:sib + 1, :]).then_inc(S["s_pl"], 16)
                            E.wait_ge(S["s_pl"], pollc)
                            E.reg_add(pollc, pollc, 16)
                            return E.value_load(sb_flag) != nonce

                        with E.While(cond):
                            E.nop()
                        E.dma_start(out=s_pp, in_=sh_s[sib]).then_inc(S["s_x"], 16)
                if DEBUG_TAPS:
                    E.reg_save(sb_flag, pollc)
                    E.dma_start(out=nc.dram_tensor("d_polls", [1, 1], I32,
                                                   kind="ExternalOutput").ap(),
                                in_=sb_flag).then_inc(S["s_pl"], 16)

        b.emit("gpsimd", f_xch)
        b.wait("vector", "s_x", gv3)
        v_sa = vop(lambda E: E.tensor_add(s_sb, s_sb, s_pp))
        vz = v_sa
        for hp in range(NFT):
            vz1 = vop(lambda E, hp=hp: E.memset(s_sb[0:64, hp * 128 + 64: hp * 128 + 128], 0.0),
                      deps=(v_sa,))
            vz = vop(lambda E, hp=hp: E.memset(s_sb[64:128, hp * 128: hp * 128 + 64], 0.0),
                     deps=(vz1,))
        b.wait("scalar", "s_v", vz)
        sbf_done = b.inc("s_s", 1)

        def f_sbf(E, S):
            E.copy(s_bf, s_sb).then_inc(S["s_s"], 1)

        b.emit("scalar", f_sbf)

        qb_ev = {}
        out_ev = {}
        out_mm = {}
        for hp in range(NFT):
            b.wait("tensor", "s_s", wq_done)
            if hp == 0:
                b.wait("tensor", "s_s", dev[(3, NFT - 1)])
                b.wait("tensor", "s_s", sev[NFT - 1])
            if hp >= 1:
                b.wait("tensor", "s_s", qb_ev[hp - 1])
            if hp >= 2:
                b.wait("tensor", "s_t", out_mm[hp - 2])
            tqv = b.inc("s_t", 1)

            def f_qp(E, S, hp=hp):
                last = None
                for ch in range(4):
                    psl = pA[:, ch * 512:(ch + 1) * 512]
                    for ft in range(NFT):
                        last = E.matmul(psl, WQ(ft, hp),
                                        xnT[:, ft * 2048 + ch * 512: ft * 2048 + ch * 512 + 512],
                                        start=(ft == 0), stop=(ft == NFT - 1))
                last.then_inc(S["s_t"], 1)

            b.emit("tensor", f_qp)
            b.wait("scalar", "s_t", tqv)
            qev = b.inc("s_s", 1)
            qb_ev[hp] = qev
            qslot = w13b[:, (hp % 2) * 2048:(hp % 2) * 2048 + 2048]

            def f_qe(E, S, qslot=qslot):
                ins = None
                for ch in range(4):
                    ins = E.copy(qslot[:, ch * 512:(ch + 1) * 512], pA[:, ch * 512:(ch + 1) * 512])
                ins.then_inc(S["s_s"], 1)

            b.emit("scalar", f_qe)
            b.wait("tensor", "s_s", qev)
            b.wait("tensor", "s_s", sbf_done)
            if hp >= 1:
                b.wait("tensor", "s_s", out_ev[hp - 1])
            omv = b.inc("s_t", 1)
            out_mm[hp] = omv

            def f_om(E, S, hp=hp, qslot=qslot):
                last = None
                for ch in range(4):
                    last = E.matmul(pC[:, ch * 512:(ch + 1) * 512],
                                    s_bf[:, hp * 128:(hp + 1) * 128],
                                    qslot[:, ch * 512:(ch + 1) * 512],
                                    start=True, stop=True)
                last.then_inc(S["s_t"], 1)

            b.emit("tensor", f_om)
            b.wait("scalar", "s_t", omv)
            oev = b.inc("s_s", 1)
            out_ev[hp] = oev

            def f_oe(E, S, hp=hp):
                ins = None
                for ch in range(4):
                    ins = E.copy(actT[:, hp * 2048 + ch * 512: hp * 2048 + ch * 512 + 512],
                                 pC[:, ch * 512:(ch + 1) * 512])
                ins.then_inc(S["s_s"], 1)

            b.emit("scalar", f_oe)
        aout_done = out_ev[NFT - 1]

        attn_resid = {0: 0, 1: 0}
        wv_last = 0
        for tt in range(NTT):
            b.wait("tensor", "s_s", aout_done)
            b.wait("tensor", "s_s", wo_done)
            if tt >= 2:
                b.wait("tensor", "s_v", attn_resid[tt % 2])
            wv_last = b.inc("s_t", 1)

            def f_wo(E, S, tt=tt):
                P = pA if tt % 2 == 0 else pC
                last = None
                for ofh in range(2):
                    psl = P[:, ofh * 512:(ofh + 1) * 512]
                    for hp in range(NFT):
                        last = E.matmul(psl,
                                        actT[:, hp * 2048 + tt * 128: hp * 2048 + (tt + 1) * 128],
                                        WO(hp, ofh),
                                        start=(hp == 0), stop=(hp == NFT - 1))
                last.then_inc(S["s_t"], 1)

            b.emit("tensor", f_wo)
            b.wait("vector", "s_t", wv_last)
            b.wait("vector", "s_v", x_last_v[tt])
            rv = b.inc("s_v", 1)
            attn_resid[tt % 2] = rv
            x_last_v[tt] = rv

            def f_ar(E, S, tt=tt):
                P = pA if tt % 2 == 0 else pC
                E.tensor_tensor(x_tile(tt), x_tile(tt), P[:, 0:1024], ALU.add).then_inc(S["s_v"], 1)

            b.emit("vector", f_ar)
        return max(attn_resid[0], attn_resid[1]), wv_last

    ln1_done = emit_ln(0)
    emit_gate(ln1_done, 0)
    ffn1_resid, ffn1_mm = emit_ffn(0)

    final_resid = ffn1_resid
    if stage >= 2:
        ln2_done = emit_ln(dict(x_last_v), to_dram=True)
        attn_resid, attn_mm = emit_attn(ln2_done)
        final_resid = attn_resid
    if stage >= 3:
        ln3_done = emit_ln(dict(x_last_v))
        emit_gate(ln3_done, attn_resid)
        ffn2_resid, ffn2_mm = emit_ffn(1)
        final_resid = ffn2_resid

    if DEBUG_TAPS:
        d_xnT = nc.dram_tensor("d_xnT", [128, NFT * 2048], BF16, kind="ExternalOutput").ap()
        d_gate = nc.dram_tensor("d_gate", [128, 128], F32, kind="ExternalOutput").ap()
        d_comb = nc.dram_tensor("d_comb", [128, 128], F32, kind="ExternalOutput").ap()
        d_act = nc.dram_tensor("d_act", [128, NJP * 1024], BF16, kind="ExternalOutput").ap()
        b.wait("gpsimd", "s_v", final_resid)
        b.wait("gpsimd", "s_s", b.cnt["s_s"])
        for (dst, src) in [(d_xnT, xnT), (d_gate, gate_sb), (d_comb, combine), (d_act, actT)]:
            gvd = b.inc("s_st", 16)

            def f_dbg(E, S, dst=dst, src=src):
                E.dma_start(out=dst, in_=src).then_inc(S["s_st"], 16)

            b.emit("gpsimd", f_dbg)

    b.wait("gpsimd", "s_v", final_resid)
    y3 = y_out.rearrange("(t p) f -> t p f", p=128)
    stv = 0
    for g in range(4):
        stv = b.inc("s_st", 16)

        def f_st(E, S, g=g):
            E.dma_start(out=y3[4 * g:4 * g + 4].rearrange("t p f -> p t f"),
                        in_=xres3[:, 4 * g:4 * g + 4, :]).then_inc(S["s_st"], 16)

        b.emit("gpsimd", f_st)

    def f_fin(E, S, stv=stv):
        E.wait_ge(S["s_st"], stv)

    b.emit("gpsimd", f_fin)

    semnames = list(SEMS) + ["s_pl"]
    with nc.Block() as block:
        S = {nm: nc.alloc_semaphore(nm) for nm in semnames}

        @block.sync
        def _(eng):
            for f in b.ops["sync"]:
                f(eng, S)

        @block.tensor
        def _(eng):
            for f in b.ops["tensor"]:
                f(eng, S)

        @block.scalar
        def _(eng):
            for f in b.ops["scalar"]:
                f(eng, S)

        @block.vector
        def _(eng):
            for f in b.ops["vector"]:
                f(eng, S)

        @block.gpsimd
        def _(eng):
            for f in b.ops["gpsimd"]:
                f(eng, S)

    return nc


def _prep_weights(Wq, Wk, Wv, Wo, gate_w, w13, w2):
    w13t = np.ascontiguousarray(
        w13.reshape(8, 2, 16, 128, 8, 128).transpose(0, 2, 5, 1, 4, 3))
    w2t = np.ascontiguousarray(w2.transpose(0, 2, 1).reshape(8, 16, 128, 1024))
    return {
        "w13t": w13t,
        "w2t": w2t,
        "wqt": np.ascontiguousarray(Wq.T),
        "wkt": np.ascontiguousarray(Wk.T),
        "wvt": np.ascontiguousarray(Wv.T),
        "wot": np.ascontiguousarray(Wo.T),
        "gwt": np.ascontiguousarray(gate_w.T),
    }


def kernel(L0_attn_h, L0_attn_key_pe, L0_attn_Wq, L0_attn_Wk, L0_attn_Wv,
           L0_attn_Wo, L0_ffn_gate_w, L0_ffn_w13_weight, L0_ffn_w2_weight):
    h = np.asarray(L0_attn_h, dtype=np.float32)
    _ = np.asarray(L0_attn_key_pe)
    W = _prep_weights(np.asarray(L0_attn_Wq, np.float32), np.asarray(L0_attn_Wk, np.float32),
                      np.asarray(L0_attn_Wv, np.float32), np.asarray(L0_attn_Wo, np.float32),
                      np.asarray(L0_ffn_gate_w, np.float32),
                      np.asarray(L0_ffn_w13_weight, np.float32),
                      np.asarray(L0_ffn_w2_weight, np.float32))
    nonce = np.random.randint(1, 2 ** 30, size=(1, 1)).astype(np.int32)
    in_maps = []
    for c in range(8):
        bidx, half = c // 2, c % 2
        xs = np.ascontiguousarray(h[bidx, half * T:(half + 1) * T, :])
        m = {"x_in": xs, "nonce": nonce}
        m.update(W)
        in_maps.append(m)
    nc = build_nc(STAGE)
    res = run_bass_kernel_spmd(nc, in_maps, core_ids=list(range(8)))
    global LAST_RES
    LAST_RES = res
    out = np.empty((4, 4096, 1024), dtype=np.float32)
    for c in range(8):
        bidx, half = c // 2, c % 2
        out[bidx, half * T:(half + 1) * T, :] = res.results[c]["y_out"]
    return out
